# version 01
# baseline (speedup 1.0000x reference)
import numpy as np
import ml_dtypes
from contextlib import ExitStack
import concourse.bass as bass
import concourse.mybir as mybir
from concourse.bass_utils import run_bass_kernel_spmd

F32 = mybir.dt.float32
BF16 = mybir.dt.bfloat16
AF = mybir.ActivationFunctionType
ALU = mybir.AluOpType
AX = mybir.AxisListType

NCORE = 8
BATCH, SEQ, DM = 4, 8192, 2048
DIN, NH, DH, DS = 3072, 48, 64, 128
DPROJ = 6448
EPS = 1e-5
NTOK = 4096
LEAD = 512
PTOK = LEAD + NTOK
NCH = PTOK // 128
LCH = LEAD // 128
NKD = DM // 128
NCC = 24
SINTER = 5632
MOEI = 1408
NEXP = 8
GUCOLS = 2 * SINTER + NEXP * 2 * MOEI
HCAT = SINTER + NEXP * MOEI
NHC = HCAT // 128
BIG = 1.0e30
NHS = 6

_PHASES = 5
_DBG = []
_P2_STOP = None
_P2_KEEP = False


def p1_slots():
    out = []
    slot = zslot = xcslot = 0
    for t3 in range(3):
        for c in range(51):
            for ts in range(3):
                pt0 = t3 * 1536 + ts * 512
                if c < NCC:
                    if pt0 < LEAD:
                        continue
                    out.append(dict(t3=t3, c=c, ts=ts, pt0=pt0, kind="z",
                                    slot=slot, zslot=zslot, cc=c))
                    zslot += 1
                elif c < 2 * NCC:
                    out.append(dict(t3=t3, c=c, ts=ts, pt0=pt0, kind="xc",
                                    slot=slot, xcslot=xcslot, cc=c - NCC))
                    xcslot += 1
                elif c == 48:
                    out.append(dict(t3=t3, c=c, ts=ts, pt0=pt0, kind="B",
                                    slot=slot))
                elif c == 49:
                    out.append(dict(t3=t3, c=c, ts=ts, pt0=pt0, kind="C",
                                    slot=slot))
                else:
                    out.append(dict(t3=t3, c=c, ts=ts, pt0=pt0, kind="dt",
                                    slot=slot))
                slot += 1
    return out


SLOTS = p1_slots()
N_SLOTS = len(SLOTS)


def p1_cols():
    done = {}
    for s in SLOTS:
        done[(s["t3"], s["c"])] = s["slot"] + 1
    out = []
    w = 0
    for t3 in range(3):
        for c in range(51):
            if (t3, c) in done:
                out.append(dict(t3=t3, c=c, wslot=w, mm_after=done[(t3, c)]))
                w += 1
    return out


COLS = p1_cols()
COL_OF = {(c["t3"], c["c"]): c for c in COLS}


def build_program(dbg=(), phases=None):
    if phases is None:
        phases = _PHASES
    nc = bass.Bass("TRN2", target_bir_lowering=False)
    es = ExitStack()

    d_in = {}

    def I(name, shape, dt=F32):
        d_in[name] = nc.dram_tensor(name, shape, dt, kind="ExternalInput").ap()
        return d_in[name]

    XT = I("xt", [DM, PTOK])
    WIN = I("win", [DM, DPROJ])
    WOUT = I("wout", [DIN, DM])
    WUP = I("wup", [DM, GUCOLS])
    WDN = I("wdn", [HCAT, DM])
    WG9 = I("wg9", [DM, 9])
    NORMW = I("normw", [128, NKD])
    INNERW = I("innerw", [128, NCC])
    CONVW = I("convw", [128, NCC * 4])
    CONVB = I("convb", [128, NCC])
    CONVI = I("convi", [128, NCC * 3])
    DTB = I("dtb", [48, 1])
    ALOGC = I("alogc", [48, 1])
    ALOG_BC = I("alogbc", [128, 48])
    D_BCI = I("dbcin", [128, 48])
    BASE = I("basein", [128, NH * DH])
    IDENT_B = I("identb", [128, 128], BF16)
    IDENT_F = I("identf", [128, 128])
    MASK4 = I("mask4", [128, 512])
    ONEROW = I("onerow", [1, 128])
    ONECOL_B = I("onecolb", [128, 1], BF16)
    IDX8 = I("idx8", [128, 8])

    OUT = nc.dram_tensor("out", [NKD, 128, NTOK], F32, kind="ExternalOutput").ap()

    XC = nc.dram_tensor("xc_d", [NCC, 128, PTOK], BF16, kind="Internal").ap()
    SZ = nc.dram_tensor("sz_d", [NCC, 128, NTOK], BF16, kind="Internal").ap()
    SD = nc.dram_tensor("sd_d", [48, PTOK], F32, kind="Internal").ap()
    NASED = nc.dram_tensor("nased_d", [NCH, 48], F32, kind="Internal").ap()
    YR = nc.dram_tensor("yr_d", [NCC, 128, NTOK], BF16, kind="Internal").ap()
    X2 = nc.dram_tensor("x2_d", [NKD, 128, NTOK], F32, kind="Internal").ap()
    X2N = nc.dram_tensor("x2n_d", [NKD, 128, NTOK], BF16, kind="Internal").ap()
    W9D = nc.dram_tensor("w9_d", [9, NTOK], F32, kind="Internal").ap()
    HD = nc.dram_tensor("h_d", [NHC, 128, NTOK], BF16, kind="Internal").ap()
    W2C = nc.dram_tensor("w2c_d", [HCAT, DM], BF16, kind="Internal").ap()
    WUPC = nc.dram_tensor("wupc_d", [DM, GUCOLS], BF16, kind="Internal").ap()
    JUNK = nc.dram_tensor("junk_d", [16, 16], F32, kind="Internal").ap()

    dbg = list(dbg)
    dout = {}
    for name, shape, dt in [("xc", [NCC, 128, PTOK], BF16),
                            ("sz", [NCC, 128, NTOK], BF16),
                            ("sfm", [48, PTOK], F32),
                            ("bfm", [128, PTOK], BF16),
                            ("cfm", [128, PTOK], BF16),
                            ("nase", [128, NCH * 48], F32),
                            ("yr", [NCC, 128, NTOK], BF16),
                            ("x2", [NKD, 128, NTOK], F32),
                            ("x2n", [NKD, 128, NTOK], BF16),
                            ("w9", [9, NTOK], F32),
                            ("hd", [NHC, 128, NTOK], BF16)]:
        if name in dbg:
            dout[name] = nc.dram_tensor("dbg_" + name, shape, dt,
                                        kind="ExternalOutput").ap()
    P2SHAPES = dict(X_tm=([128, DIN], BF16), Xd=([128, DIN], BF16),
                    B_tm=([128, 128], BF16), s_tm=([128, 48], F32),
                    t_sn=([128, 48], F32), dcol=([128, 48], F32),
                    eas=([128, 48], F32), dfold=([128, 48], F32),
                    S_f32=([128, DIN], F32), S_bf=([128, DIN], BF16),
                    sbm0=([128, 512], F32), aD0=([128, 512], F32),
                    lb0=([128, 512], F32), MT0=([128, 512], BF16),
                    MT11=([128, 512], BF16), y_tm=([128, DIN], F32),
                    ybf=([128, DIN], BF16), srows=([128, 4], F32),
                    srow=([1, NH * 128], F32))
    for name, (shape, dt) in P2SHAPES.items():
        if name in dbg:
            dout[name] = nc.dram_tensor("dbg_" + name, shape, dt,
                                        kind="ExternalOutput").ap()

    sbt = lambda n, shp, dt=F32: es.enter_context(nc.sbuf_tensor(n, shp, dt))

    identb = sbt("identb_s", [128, 128], BF16)
    identf = sbt("identf_s", [128, 128])
    mask4 = sbt("mask4_s", [128, 512])
    onerow = sbt("onerow_s", [1, 128])
    onecolb = sbt("onecolb_s", [128, 1], BF16)
    normw = sbt("normw_s", [128, NKD])
    innerw = sbt("innerw_s", [128, NCC])
    convw = sbt("convw_s", [128, NCC * 4])
    convb = sbt("convb_s", [128, NCC])
    convi = sbt("convi_s", [128, NCC * 3])
    dtb = sbt("dtb_s", [48, 1])
    alogc = sbt("alogc_s", [48, 1])
    ea48 = sbt("ea48_s", [48, 1])
    nabc = sbt("nabc_s", [128, 48])
    abc = sbt("abc_s", [128, 48])
    dbcs = sbt("dbc_s", [128, 48])
    idx8s = sbt("idx8_s", [128, 8])

    esA = ExitStack()
    sbtA = lambda n, shp, dt=F32: esA.enter_context(nc.sbuf_tensor(n, shp, dt))
    B_fm = sbtA("b_fm", [128, PTOK], BF16)
    C_fm = sbtA("c_fm", [128, PTOK], BF16)
    S_fm = sbtA("s_fm", [48, PTOK])
    nase = sbtA("nase_s", [128, NCH * 48])
    zeros48 = sbtA("zeros48", [128, 48])

    env = dict(locals())

    with nc.Block() as block:
        sem = {}

        def S(name):
            if name not in sem:
                sem[name] = es.enter_context(nc.semaphore(name))
            return sem[name]

        PH = S("ph")
        env["S"] = S
        env["block"] = block

        @block.sync
        def _(sync):
            for dst, src in [(identb, IDENT_B), (identf, IDENT_F),
                             (mask4, MASK4), (onerow, ONEROW),
                             (onecolb, ONECOL_B), (normw, NORMW),
                             (innerw, INNERW), (convw, CONVW),
                             (convb, CONVB), (convi, CONVI), (dtb, DTB),
                             (alogc, ALOGC), (dbcs, D_BCI), (idx8s, IDX8),
                             (nabc, ALOG_BC)]:
                sync.dma_start(out=dst[:], in_=src).then_inc(PH, 16)

        @block.gpsimd
        def _(gpsimd):
            if phases >= 5:
                gpsimd.dma_start(out=W2C, in_=WDN).then_inc(S("w2c"), 16)
            if phases >= 4:
                gpsimd.dma_start(out=WUPC, in_=WUP).then_inc(S("wupc"), 16)

        @block.scalar
        def _(scalar):
            scalar.wait_ge(PH, 15 * 16)
            scalar.activation(ea48[:], alogc[:], AF.Exp)
            scalar.activation(nabc[:], nabc[:], AF.Exp).then_inc(PH, 1)

        @block.vector
        def _(vector):
            vector.wait_ge(PH, 15 * 16 + 1)
            vector.memset(zeros48[:], 0.0)
            vector.tensor_scalar(abc[:], nabc[:], -1.0, None,
                                 ALU.mult).then_inc(PH, 1)

        PH_P0 = 15 * 16 + 2
        env["PH_P0"] = PH_P0
        ph_end = [PH_P0]

        if phases >= 1:
            ph_end[0] = emit_p1(nc, env, ph_end[0])
        if phases >= 2:
            ph_end[0] = emit_p2(nc, env, ph_end[0])
        if phases >= 3:
            esA.close()
            env["esA_closed"] = True
            ph_end[0] = emit_p3(nc, env, ph_end[0])
        if phases >= 4:
            ph_end[0] = emit_p4(nc, env, ph_end[0])
        if phases >= 5:
            ph_end[0] = emit_p5(nc, env, ph_end[0])

        if dout:
            @block.gpsimd
            def _(gpsimd):
                gpsimd.wait_ge(PH, ph_end[0])
                srcs = {"xc": XC, "sz": SZ, "sfm": S_fm[:], "bfm": B_fm[:],
                        "cfm": C_fm[:], "nase": nase[:], "yr": YR, "x2": X2,
                        "x2n": X2N, "w9": W9D, "hd": HD}
                for bn, buf in env.get("p2bufs", {}).items():
                    srcs[bn] = buf[:]
                n = 0
                for name, dst in dout.items():
                    gpsimd.dma_start(out=dst, in_=srcs[name]).then_inc(S("dbst"), 16)
                    n += 1
                gpsimd.wait_ge(S("dbst"), 16 * n)

        if phases < 5:
            @block.sync
            def _(sync):
                sync.wait_ge(PH, ph_end[0])
                sync.dma_start(out=OUT[0], in_=XT[0:128, 0:NTOK]).then_inc(S("fin"), 16)
                sync.wait_ge(S("fin"), 16)
        else:
            @block.sync
            def _(sync):
                sync.wait_ge(S("p5out"), 16 * (NTOK // 512))

    return nc, d_in


def emit_p1(nc, env, ph0):
    block = env["block"]
    S = env["S"]
    PH = S("ph")
    XT, WIN, XC, SZ, SD, NASED = (env["XT"], env["WIN"], env["XC"], env["SZ"],
                                  env["SD"], env["NASED"])
    B_fm, C_fm, S_fm, nase = env["B_fm"], env["C_fm"], env["S_fm"], env["nase"]
    normw, convw, convb, convi, dtb, ea48 = (
        env["normw"], env["convw"], env["convb"], env["convi"], env["dtb"],
        env["ea48"])
    onerow, onecolb, identf = env["onerow"], env["onecolb"], env["identf"]

    p1 = ExitStack()
    sbt = lambda n, shp, dt=F32: p1.enter_context(nc.sbuf_tensor(n, shp, dt))
    pst = lambda n, shp, dt=F32: p1.enter_context(nc.psum_tensor(n, shp, dt))
    xtile = sbt("p1_x", [128, NKD * 256])
    xbf = sbt("p1_xbf", [128, NKD * 256], BF16)
    xsq = sbt("p1_xsq", [128, NKD * 256], BF16)
    xn = sbt("p1_xn", [128, NKD * 1536], BF16)
    rows = sbt("p1_rows", [1, 256 * 8])
    wbuf = [sbt(f"p1_w{i}", [128, NKD * 128], BF16) for i in range(2)]
    zbuf = [sbt(f"p1_z{i}", [128, 512], BF16) for i in range(2)]
    xcp = sbt("p1_xcp", [128, 515])
    cvo = sbt("p1_cvo", [128, 512])
    xcs = [sbt(f"p1_xcs{i}", [128, 512], BF16) for i in range(2)]
    tails = sbt("p1_tails", [128, NCC * 3])
    t48a = sbt("p1_t48a", [48, 512])
    t48b = sbt("p1_t48b", [48, 512])
    t48c = sbt("p1_t48c", [48, 512])
    nasef = sbt("p1_nasef", [48, NCH])
    ones48 = sbt("p1_ones48", [48, 512])
    nrow = sbt("p1_nrow", [1, NCH * 48])
    ps_sx = pst("p1_psx", [1, 256])
    ps_sq = pst("p1_psq", [1, 256])
    ps_m = pst("p1_pm", [128, 256])
    ps_r = pst("p1_pr", [128, 256])
    ps_nt = pst("p1_pnt", [48, 48])
    pp = [pst(f"p1_pp{i}", [128, 512]) for i in range(2)]

    RM, RMSQ, RVAR, RI1, RR1, RVR, RRS, RT = range(8)

    def row(i):
        return rows[0:1, i * 256:(i + 1) * 256]

    NSEG = PTOK // 512

    @block.sync
    def _(sync):
        sync.wait_ge(PH, ph0)
        for t3 in range(3):
            for hs in range(NHS):
                gi = t3 * NHS + hs
                if gi > 0:
                    sync.wait_ge(S("p1xn"), gi)
                src = XT.rearrange("(k p) t -> p k t", p=128)
                c0 = t3 * 1536 + hs * 256
                sync.dma_start(out=xtile[:].rearrange("p (k t) -> p k t", t=256),
                               in_=src[:, :, c0:c0 + 256]).then_inc(S("p1x"), 16)
            for s in [s for s in SLOTS if s["t3"] == t3]:
                if s["kind"] == "z":
                    sync.wait_ge(S("p1ev"), s["slot"] + 1)
                    sync.dma_start(
                        out=SZ[s["cc"], :, s["pt0"] - LEAD:s["pt0"] - LEAD + 512],
                        in_=zbuf[s["zslot"] % 2][:]).then_inc(S("p1zs"), 16)
                elif s["kind"] == "xc":
                    sync.wait_ge(S("p1cv"), s["xcslot"] + 1)
                    sync.dma_start(
                        out=XC[s["cc"], :, s["pt0"]:s["pt0"] + 512],
                        in_=xcs[s["xcslot"] % 2][:]).then_inc(S("p1xcs"), 16)
        sync.wait_ge(S("p1c"), 1)
        sync.dma_start(out=SD, in_=S_fm[:]).then_inc(S("p1dma"), 16)
        sync.wait_ge(S("p1nbv"), 1)
        sync.dma_start(out=NASED, in_=t48a[0:36, 0:48]).then_inc(S("p1dma"), 16)
        sync.wait_ge(S("p1dma"), 32)
        sync.dma_start(out=nrow[:],
                       in_=bass.AP(NASED.tensor, 0, [[NCH * 48, 1], [1, NCH * 48]])
                       ).then_inc(S("p1dma"), 16)

    @block.gpsimd
    def _(gpsimd):
        gpsimd.wait_ge(PH, ph0)
        for col in COLS:
            b = col["wslot"] % 2
            if col["wslot"] >= 2:
                gpsimd.wait_ge(S("p1mm"), COLS[col["wslot"] - 2]["mm_after"])
            ncols = 48 if col["c"] == 50 else 128
            src = WIN.rearrange("(k p) n -> p k n", p=128)
            dst = wbuf[b][:].rearrange("p (k n) -> p k n", n=128)
            gpsimd.dma_start(
                out=dst[:, :, 0:ncols],
                in_=src[:, :, col["c"] * 128:col["c"] * 128 + ncols]
            ).then_inc(S("p1w"), 16)

    @block.scalar
    def _(scalar):
        scalar.wait_ge(PH, ph0)
        for t3 in range(3):
            for hs in range(NHS):
                gi = t3 * NHS + hs
                scalar.wait_ge(S("p1x"), 16 * (gi + 1))
                if gi > 0:
                    scalar.wait_ge(S("p1st"), gi)
                scalar.activation(xsq[:], xtile[:], AF.Square).then_inc(S("p1sq"), 1)
                scalar.wait_ge(S("p1v1"), gi + 1)
                scalar.activation(row(RR1), row(RI1), AF.Ln)
                scalar.activation(row(RR1), row(RR1), AF.Exp, scale=0.5)
                scalar.activation(row(RT), row(RVR), AF.Ln)
                scalar.activation(row(RT), row(RT), AF.Exp,
                                  scale=0.5).then_inc(S("p1s1"), 1)
            for s in [s for s in SLOTS if s["t3"] == t3]:
                slot, par = s["slot"], s["slot"] % 2
                scalar.wait_ge(S("p1mm"), slot + 1)
                if s["kind"] == "z":
                    if s["zslot"] >= 2:
                        scalar.wait_ge(S("p1zs"), 16 * (s["zslot"] - 1))
                    scalar.activation(zbuf[s["zslot"] % 2][:], pp[par][:],
                                      AF.Silu).then_inc(S("p1ev"), 1)
                elif s["kind"] == "xc":
                    xs = s["xcslot"]
                    if xs >= 1:
                        scalar.wait_ge(S("p1cvv"), 2 * xs)
                    scalar.activation(xcp[:, 3:515], pp[par][:],
                                      AF.Copy).then_inc(S("p1ev"), 1)
                    scalar.wait_ge(S("p1cvv"), 2 * xs + 1)
                    if xs >= 2:
                        scalar.wait_ge(S("p1xcs"), 16 * (xs - 1))
                    scalar.activation(xcs[xs % 2][:], cvo[:], AF.Silu,
                                      bias=convb[:, s["cc"]:s["cc"] + 1]
                                      ).then_inc(S("p1cv"), 1)
                elif s["kind"] == "B":
                    scalar.activation(B_fm[:, s["pt0"]:s["pt0"] + 512],
                                      pp[par][:], AF.Copy).then_inc(S("p1ev"), 1)
                elif s["kind"] == "C":
                    scalar.activation(C_fm[:, s["pt0"]:s["pt0"] + 512],
                                      pp[par][:], AF.Copy).then_inc(S("p1ev"), 1)
                elif s["kind"] == "dt":
                    scalar.activation(S_fm[:, s["pt0"]:s["pt0"] + 512],
                                      pp[par][0:48, :],
                                      AF.Copy).then_inc(S("p1ev"), 1)
        for seg in range(NSEG):
            scalar.wait_ge(S("p1sp"), 2 * seg + 1)
            scalar.activation(t48b[:], t48a[:], AF.Exp, scale=-1.0)
            scalar.activation(t48b[:], t48b[:], AF.Ln,
                              bias=1.0).then_inc(S("p1sp"), 1)

    @block.vector
    def _(vector):
        vector.wait_ge(PH, ph0)
        vector.memset(tails[:], 0.0)
        vector.memset(ones48[:], 1.0)
        for t3 in range(3):
            for hs in range(NHS):
                gi = t3 * NHS + hs
                vector.wait_ge(S("p1x"), 16 * (gi + 1))
                if gi > 0:
                    vector.wait_ge(S("p1st"), gi)
                if t3 > 0 and hs == 0:
                    vector.wait_ge(S("p1mm"), COL_OF[(t3 - 1, 50)]["mm_after"])
                vector.tensor_copy(xbf[:], xtile[:]).then_inc(S("p1xb"), 1)
                vector.wait_ge(S("p1st"), gi + 1)
                if gi > 0:
                    vector.wait_ge(S("p1bc"), gi)
                vector.tensor_scalar(row(RM), ps_sx[0:1, :], 1.0 / DM, None,
                                     ALU.mult)
                vector.tensor_scalar(row(RMSQ), ps_sq[0:1, :], 1.0 / DM, None,
                                     ALU.mult)
                vector.tensor_tensor(row(RT), row(RM), row(RM), ALU.mult)
                vector.tensor_tensor(row(RVAR), row(RMSQ), row(RT), ALU.subtract)
                vector.tensor_scalar(row(RI1), row(RVAR), EPS, None, ALU.add)
                vector.reciprocal(row(RI1), row(RI1))
                vector.tensor_tensor(row(RVR), row(RVAR), row(RI1), ALU.mult)
                vector.tensor_scalar(row(RVR), row(RVR), EPS, None, ALU.add)
                vector.reciprocal(row(RVR), row(RVR)).then_inc(S("p1v1"), 1)
                vector.wait_ge(S("p1s1"), gi + 1)
                vector.tensor_tensor(row(RRS), row(RR1), row(RT),
                                     ALU.mult).then_inc(S("p1v2"), 1)
                vector.wait_ge(S("p1bc"), gi + 1)
                for k in range(NKD):
                    xo = k * 1536 + hs * 256
                    vector.tensor_tensor(xtile[:, k * 256:(k + 1) * 256],
                                         xtile[:, k * 256:(k + 1) * 256],
                                         ps_m[:], ALU.subtract)
                    inst = vector.scalar_tensor_tensor(
                        xn[:, xo:xo + 256], xtile[:, k * 256:(k + 1) * 256],
                        normw[:, k:k + 1], ps_r[:], ALU.mult, ALU.mult)
                inst.then_inc(S("p1xn"), 1)
            for s in [s for s in SLOTS if s["t3"] == t3]:
                slot = s["slot"]
                if s["kind"] == "xc":
                    cc, xs = s["cc"], s["xcslot"]
                    vector.wait_ge(S("p1ev"), slot + 1)
                    if xs >= 1:
                        vector.wait_ge(S("p1cv"), xs)
                    vector.tensor_copy(xcp[:, 0:3], tails[:, cc * 3:cc * 3 + 3])
                    vector.tensor_scalar(cvo[:], xcp[:, 0:512],
                                         convw[:, cc * 4:cc * 4 + 1], None,
                                         ALU.mult)
                    for kk in range(1, 4):
                        inst = vector.scalar_tensor_tensor(
                            cvo[:], xcp[:, kk:kk + 512],
                            convw[:, cc * 4 + kk:cc * 4 + kk + 1], cvo[:],
                            ALU.mult, ALU.add)
                    inst.then_inc(S("p1cvv"), 1)
                    if s["t3"] == 0 and s["ts"] == 0:
                        vector.tensor_tensor(tails[:, cc * 3:cc * 3 + 3],
                                             xcp[:, 512:515],
                                             convi[:, cc * 3:cc * 3 + 3],
                                             ALU.add).then_inc(S("p1cvv"), 1)
                    else:
                        vector.tensor_copy(tails[:, cc * 3:cc * 3 + 3],
                                           xcp[:, 512:515]
                                           ).then_inc(S("p1cvv"), 1)
        vector.wait_ge(S("p1ev"), N_SLOTS)
        for seg in range(NSEG):
            sl = S_fm[:, seg * 512:(seg + 1) * 512]
            vector.tensor_scalar(sl, sl, dtb[:], None, ALU.add)
            vector.tensor_scalar(t48a[:], sl, -1.0, None, ALU.mult)
            vector.tensor_tensor(t48a[:], t48a[:], sl,
                                 ALU.max).then_inc(S("p1sp"), 1)
            vector.tensor_relu(t48c[:], sl)
            vector.wait_ge(S("p1sp"), 2 * (seg + 1))
            vector.tensor_tensor(sl, t48c[:], t48b[:], ALU.add)
        for seg in range(NSEG):
            sl = S_fm[:, seg * 512:(seg + 1) * 512]
            vector.tensor_tensor_scan(t48c[:], ones48[:], sl, 0.0,
                                      ALU.mult, ALU.add)
            if seg == 0:
                vector.tensor_copy(sl, t48c[:])
            else:
                vector.tensor_scalar(sl, t48c[:],
                                     S_fm[:, seg * 512 - 1:seg * 512], None,
                                     ALU.add)
        vector.tensor_scalar(nasef[:], S_fm[:, 127::128], ea48[:], None,
                             ALU.mult).then_inc(S("p1c"), 1)
        vector.wait_ge(S("p1nb"), 1)
        vector.tensor_copy(t48a[0:36, 0:48], ps_nt[0:36, :]).then_inc(S("p1nbv"), 1)
        for i in range(4):
            n = min(512, NCH * 48 - i * 512)
            vector.wait_ge(S("p1nb"), 2 + i)
            vector.tensor_copy(nase[:, i * 512:i * 512 + n],
                               ps_m[:, 0:256] if False else
                               (pp[i % 2][:, 0:n])).then_inc(S("p1nbv"), 1)

    @block.tensor
    def _(tensor):
        tensor.wait_ge(PH, ph0)
        for t3 in range(3):
            for hs in range(NHS):
                gi = t3 * NHS + hs
                tensor.wait_ge(S("p1sq"), gi + 1)
                tensor.wait_ge(S("p1xb"), gi + 1)
                if gi > 0:
                    tensor.wait_ge(S("p1v1"), gi)
                for k in range(NKD):
                    tensor.matmul(ps_sx[:], onecolb[:],
                                  xbf[:, k * 256:(k + 1) * 256],
                                  start=(k == 0), stop=(k == NKD - 1))
                for k in range(NKD):
                    inst = tensor.matmul(ps_sq[:], onecolb[:],
                                         xsq[:, k * 256:(k + 1) * 256],
                                         start=(k == 0), stop=(k == NKD - 1))
                inst.then_inc(S("p1st"), 1)
                tensor.wait_ge(S("p1v2"), gi + 1)
                if gi > 0:
                    tensor.wait_ge(S("p1xn"), gi)
                tensor.matmul(ps_m[:], onerow[:], row(RM), start=True, stop=True)
                tensor.matmul(ps_r[:], onerow[:], row(RRS),
                              start=True, stop=True).then_inc(S("p1bc"), 1)
            for s in [s for s in SLOTS if s["t3"] == t3]:
                slot, par = s["slot"], s["slot"] % 2
                col = COL_OF[(t3, s["c"])]
                tensor.wait_ge(S("p1w"), 16 * (col["wslot"] + 1))
                tensor.wait_ge(S("p1xn"), t3 * NHS + (s["ts"] + 1) * 2)
                if slot >= 2:
                    tensor.wait_ge(S("p1ev"), slot - 1)
                ncols = 48 if s["kind"] == "dt" else 128
                b = col["wslot"] % 2
                dst = pp[par][0:ncols, :]
                for k in range(NKD):
                    xo = k * 1536 + s["ts"] * 512
                    inst = tensor.matmul(
                        dst, wbuf[b][:, k * 128:k * 128 + ncols],
                        xn[:, xo:xo + 512],
                        start=(k == 0), stop=(k == NKD - 1))
                inst.then_inc(S("p1mm"), 1)
        tensor.wait_ge(S("p1c"), 1)
        tensor.transpose(ps_nt[0:36, :], nasef[:, 0:36],
                         identf[0:48, 0:48]).then_inc(S("p1nb"), 1)
        tensor.wait_ge(S("p1dma"), 48)
        tensor.wait_ge(S("p1ev"), N_SLOTS)
        for i in range(4):
            n = min(512, NCH * 48 - i * 512)
            if i >= 2:
                tensor.wait_ge(S("p1nbv"), i)
            tensor.matmul(pp[i % 2][:, 0:n], onerow[:],
                          nrow[0:1, i * 512:i * 512 + n],
                          start=True, stop=True).then_inc(S("p1nb"), 1)

    JUNK = env["JUNK"]

    @block.sync
    def _(sync):
        sync.wait_ge(S("p1zs"), 16 * 192)
        sync.wait_ge(S("p1xcs"), 16 * 216)
        sync.wait_ge(S("p1dma"), 48)
        sync.dma_start(out=JUNK[0:1, 0:1], in_=S_fm[0:1, 0:1]).then_inc(PH, 16)

    @block.gpsimd
    def _(gpsimd):
        gpsimd.wait_ge(S("p1w"), 16 * len(COLS))
        gpsimd.dma_start(out=JUNK[0:1, 1:2], in_=S_fm[0:1, 1:2]).then_inc(PH, 16)

    @block.vector
    def _(vector):
        vector.wait_ge(S("p1nbv"), 5)
        vector.engine_nop().then_inc(PH, 1)

    @block.scalar
    def _(scalar):
        scalar.wait_ge(S("p1sp"), 2 * NSEG)
        scalar.wait_ge(S("p1cv"), 216)
        scalar.wait_ge(S("p1ev"), N_SLOTS)
        scalar.activation(t48b[0:1, 0:1], t48b[0:1, 0:1],
                          AF.Copy).then_inc(PH, 1)

    @block.tensor
    def _(tensor):
        tensor.wait_ge(S("p1nbv"), 5)
        tensor.transpose(ps_nt[0:36, :], nasef[:, 0:36],
                         identf[0:48, 0:48]).then_inc(PH, 1)

    p1.close()
    return ph0 + 16 + 16 + 1 + 1 + 1


def _f32(a):
    return np.ascontiguousarray(np.asarray(a), dtype=np.float32)


def prep_in_maps(inputs):
    x = _f32(inputs["L0_attn_x"]).reshape(BATCH * SEQ, DM)
    xt = np.ascontiguousarray(x.T)
    win = np.ascontiguousarray(_f32(inputs["L0_attn_in_proj_w"]).T)
    wout = np.ascontiguousarray(_f32(inputs["L0_attn_out_proj_w"]).T)
    sgu = _f32(inputs["L0_ffn_shared_gate_up_w"])
    w13 = _f32(inputs["L0_ffn_w13"])
    wup = np.ascontiguousarray(
        np.concatenate([sgu.T] + [w13[e].T for e in range(NEXP)], axis=1))
    sd = _f32(inputs["L0_ffn_shared_down_w"])
    w2 = _f32(inputs["L0_ffn_w2"])
    wdn = np.ascontiguousarray(
        np.concatenate([sd.T] + [w2[e].T for e in range(NEXP)], axis=0))
    wg9 = np.ascontiguousarray(
        np.concatenate([_f32(inputs["L0_ffn_gate_w"]),
                        _f32(inputs["L0_ffn_shared_gate_w"])], axis=0).T)
    normw = np.ascontiguousarray(_f32(inputs["L0_attn_norm_w"]).reshape(NKD, 128).T)
    innerw = np.ascontiguousarray(_f32(inputs["L0_attn_inner_norm_w"]).reshape(NCC, 128).T)
    convw = np.ascontiguousarray(
        _f32(inputs["L0_attn_conv_w"]).reshape(NCC, 128, 4).transpose(1, 0, 2).reshape(128, NCC * 4))
    convb = np.ascontiguousarray(_f32(inputs["L0_attn_conv_b"]).reshape(NCC, 128).T)
    cst = _f32(inputs["L0_attn_conv_state"]).reshape(DIN, 3)
    convi = np.ascontiguousarray(cst.reshape(NCC, 128, 3).transpose(1, 0, 2).reshape(128, NCC * 3))
    dtbv = _f32(inputs["L0_attn_dt_bias"]).reshape(48, 1)
    alog = _f32(inputs["L0_attn_A_log"]).reshape(48)
    alogc = np.ascontiguousarray(alog.reshape(48, 1))
    alogbc = np.ascontiguousarray(np.broadcast_to(alog, (128, 48)))
    dbcin = np.ascontiguousarray(
        np.broadcast_to(_f32(inputs["L0_attn_D_param"]).reshape(48), (128, 48)))
    ssm = _f32(inputs["L0_attn_ssm_state"]).reshape(NH, DS, DH)
    base_even = np.ascontiguousarray(ssm.transpose(1, 0, 2).reshape(128, NH * DH))
    zeros_base = np.zeros_like(base_even)
    zeros_convi = np.zeros_like(convi)
    identb = np.eye(128, dtype=ml_dtypes.bfloat16)
    identf = np.eye(128, dtype=np.float32)
    jj, ii = np.mgrid[0:128, 0:128]
    m1 = np.where(jj > ii, BIG, 0.0).astype(np.float32)
    mask4 = np.ascontiguousarray(np.tile(m1, (1, 4)))
    onerow = np.ones((1, 128), dtype=np.float32)
    onecolb = np.ones((128, 1), dtype=ml_dtypes.bfloat16)

    in_maps = []
    for c in range(NCORE):
        t0 = c * NTOK
        xtc = np.zeros((DM, PTOK), dtype=np.float32)
        xtc[:, LEAD:] = xt[:, t0:t0 + NTOK]
        if c % 2 == 1:
            xtc[:, 0:LEAD] = xt[:, t0 - LEAD:t0]
        in_maps.append({
            "xt": xtc, "win": win, "wout": wout, "wup": wup, "wdn": wdn,
            "wg9": wg9, "normw": normw, "innerw": innerw, "convw": convw,
            "convb": convb,
            "convi": convi if c % 2 == 0 else zeros_convi,
            "dtb": dtbv, "alogc": alogc, "alogbc": alogbc, "dbcin": dbcin,
            "basein": base_even if c % 2 == 0 else zeros_base,
            "identb": identb, "identf": identf, "mask4": mask4,
            "onerow": onerow, "onecolb": onecolb,
            "idx8": np.ascontiguousarray(
                np.broadcast_to(np.arange(8, dtype=np.float32), (128, 8))),
        })
    return in_maps


def kernel(**inputs):
    nc, _ = build_program(dbg=_DBG, phases=_PHASES)
    in_maps = prep_in_maps(inputs)
    res = run_bass_kernel_spmd(nc, in_maps, core_ids=list(range(NCORE)))
    parts = [np.asarray(res.results[c]["out"], dtype=np.float32).reshape(DM, NTOK)
             for c in range(NCORE)]
    full = np.concatenate(parts, axis=1)
    return np.ascontiguousarray(full.T).reshape(BATCH, SEQ, DM)


def p2_events():
    ev = []
    pt = 0
    for cc in range(NCH):
        real = cc >= LCH
        for i in range(6):
            ev.append(dict(cc=cc, kind="pin", i=i, pt=pt)); pt += 1
        ev.append(dict(cc=cc, kind="s", pt=None))
        ev.append(dict(cc=cc, kind="B", pt=pt)); pt += 1
        if real:
            for i in range(6):
                ev.append(dict(cc=cc, kind="pout", i=i, pt=pt)); pt += 1
    for n, e in enumerate(ev):
        e["n"] = n
    return ev


P2EV = p2_events()
P2PT = [e for e in P2EV if e["pt"] is not None]


def p2_counts(cc):
    r = max(0, cc - LCH)
    return dict(
        din=3 * r + min(cc, LCH) if False else None,
        xd=6 * cc, g=r, sb=12 * r, ymm=12 * r, st=6 * cc, sbf=r,
        fin=r, yr=r,
    )


def emit_p2(nc, env, ph0):
    stop = _P2_STOP if _P2_STOP is not None else NCH
    CHS = list(range(stop))
    EV = [e for e in P2EV if e["cc"] < stop]
    NEV = len(EV)
    block = env["block"]
    S = env["S"]
    PH = S("ph")
    XC, SZ, SD, YR = env["XC"], env["SZ"], env["SD"], env["YR"]
    B_fm, C_fm, S_fm, nase = env["B_fm"], env["C_fm"], env["S_fm"], env["nase"]
    zeros48 = env["zeros48"]
    BASE = env["BASE"]
    nabc, abc, dbcs = env["nabc"], env["abc"], env["dbcs"]
    identb, identf, mask4, onerow = (env["identb"], env["identf"],
                                     env["mask4"], env["onerow"])

    p2 = ExitStack()
    sbt = lambda n, shp, dt=F32: p2.enter_context(nc.sbuf_tensor(n, shp, dt))
    pst = lambda n, shp, dt=F32: p2.enter_context(nc.psum_tensor(n, shp, dt))
    xcc = [sbt(f"p2_xcc{i}", [128, NCC * 128], BF16) for i in range(2)]
    szc = [sbt(f"p2_szc{i}", [128, NCC * 128], BF16) for i in range(2)]
    srow = sbt("p2_srow", [1, NH * 128])
    X_tm = sbt("p2_xtm", [128, DIN], BF16)
    Xd = sbt("p2_xd", [128, DIN], BF16)
    B_tm = sbt("p2_btm", [128, 128], BF16)
    s_tm = sbt("p2_stm", [128, 48])
    t_sn = sbt("p2_tsn", [128, 48])
    dtmp = sbt("p2_dtmp", [128, 3 * 48])
    dcol = sbt("p2_dcol", [128, 48])
    eas = sbt("p2_eas", [128, 48])
    dfold = sbt("p2_dfold", [128, 48])
    S_f32 = sbt("p2_sf32", [128, DIN])
    S_tmp = sbt("p2_stmp", [128, 512])
    S_bf = sbt("p2_sbf", [128, DIN], BF16)
    sbm = [sbt(f"p2_sbm{i}", [128, 512]) for i in range(2)]
    aD = [sbt(f"p2_ad{i}", [128, 512]) for i in range(2)]
    lbuf = [sbt(f"p2_lb{i}", [128, 512]) for i in range(2)]
    MT = [sbt(f"p2_mt{i}", [128, 512], BF16) for i in range(12)]
    yint = [sbt(f"p2_yi{i}", [128, 256]) for i in range(2)]
    t4 = sbt("p2_t4", [128, 256])
    y_tm = sbt("p2_ytm", [128, DIN])
    ybf = sbt("p2_ybf", [128, DIN], BF16)
    ysq = sbt("p2_ysq", [128, DIN], BF16)
    ygc = [sbt(f"p2_ygc{i}", [128, NCC * 128], BF16) for i in range(2)]
    srows = sbt("p2_srows", [128, 4])
    pT = [pst(f"p2_pt{i}", [128, 512], BF16) for i in range(2)]
    pG = pst("p2_pg", [128, 128])
    pSB = [pst(f"p2_psb{i}", [128, 512]) for i in range(2)]
    pY = [pst(f"p2_py{i}", [128, 512]) for i in range(2)]
    pSA = pst("p2_psa", [128, 512])

    ev_by_cc = {}
    for e in P2EV:
        ev_by_cc.setdefault(e["cc"], []).append(e)
    pt_to_n = {e["pt"]: e["n"] for e in P2EV if e["pt"] is not None}

    @block.sync
    def _(sync):
        sync.wait_ge(PH, ph0)
        sync.dma_start(out=y_tm[:], in_=BASE).then_inc(S("p2base"), 16)
        nin = 0
        for cc in CHS:
            real = cc >= LCH
            if cc >= 2:
                done_tp = [e["n"] + 1 for e in ev_by_cc[cc - 2]
                           if e["kind"] == "pin"][-1]
                sync.wait_ge(S("p2tp"), done_tp)
            sync.dma_start(
                out=xcc[cc % 2][:].rearrange("p (c t) -> p c t", t=128),
                in_=XC[:, :, cc * 128:(cc + 1) * 128].rearrange("c p t -> p c t")
            ).then_inc(S("p2in"), 16)
            nin += 16
            if real:
                rt0 = (cc - LCH) * 128
                if cc >= LCH + 2:
                    done_te = [e["n"] + 1 for e in ev_by_cc[cc - 2]
                               if e["kind"] == "pout"][-1]
                    sync.wait_ge(S("p2te"), done_te)
                sync.dma_start(
                    out=szc[cc % 2][:].rearrange("p (c t) -> p c t", t=128),
                    in_=SZ[:, :, rt0:rt0 + 128].rearrange("c p t -> p c t")
                ).then_inc(S("p2in"), 16)
                if cc > LCH:
                    sync.wait_ge(S("p2sb"), 12 * (cc - LCH))
                sync.dma_start(
                    out=srow[:],
                    in_=bass.AP(SD.tensor, cc * 128,
                                [[1, 1], [PTOK, NH], [1, 128]])
                ).then_inc(S("p2in"), 16)
                nin += 32

    @block.tensor
    def _(tensor):
        tensor.wait_ge(PH, ph0)
        nin = 0
        for cc in CHS:
            real = cc >= LCH
            cnt = p2_counts(cc)
            nin += 16
            myev = ev_by_cc[cc]
            for e in myev:
                if e["kind"] == "pin":
                    tensor.wait_ge(S("p2in"), nin)
                    if e["pt"] >= 2:
                        tensor.wait_ge(S("p2te"), pt_to_n[e["pt"] - 2] + 1)
                    for q in range(4):
                        fs = e["i"] * 4 + q
                        inst = tensor.transpose(
                            pT[e["pt"] % 2][:, q * 128:(q + 1) * 128],
                            xcc[cc % 2][:, fs * 128:(fs + 1) * 128], identb[:])
                    inst.then_inc(S("p2tp"), 1)
                elif e["kind"] == "s":
                    if cc >= 1:
                        prev_s = next(x for x in ev_by_cc[cc - 1]
                                      if x["kind"] == "s")
                        tensor.wait_ge(S("p2te"), prev_s["n"] + 1)
                    if real and cc > LCH:
                        tensor.wait_ge(S("p2mt"), 12 * (cc - LCH))
                    inst = tensor.transpose(
                        pG[:, 0:48], S_fm[:, cc * 128:(cc + 1) * 128],
                        identf[0:48, 0:48])
                    inst.then_inc(S("p2tp"), 1)
                elif e["kind"] == "B":
                    if e["pt"] >= 2:
                        tensor.wait_ge(S("p2te"), pt_to_n[e["pt"] - 2] + 1)
                    inst = tensor.transpose(
                        pT[e["pt"] % 2][:, 0:128],
                        B_fm[:, cc * 128:(cc + 1) * 128], identb[:])
                    inst.then_inc(S("p2tp"), 1)
            if real:
                nin += 32
                s_ev = next(e for e in myev if e["kind"] == "s")
                tensor.wait_ge(S("p2te"), s_ev["n"] + 1)
                tensor.matmul(pG[:], B_fm[:, cc * 128:(cc + 1) * 128],
                              C_fm[:, cc * 128:(cc + 1) * 128],
                              start=True, stop=True).then_inc(S("p2g"), 1)
                tensor.wait_ge(S("p2in"), nin)
                for g in range(12):
                    gi = cnt["sb"] + g
                    if gi >= 2:
                        tensor.wait_ge(S("p2sbm"), 2 * gi - 3)
                    tensor.matmul(pSB[gi % 2][:], onerow[:],
                                  srow[0:1, g * 512:(g + 1) * 512],
                                  start=True, stop=True).then_inc(S("p2sb"), 1)
            def emit_state_groups():
                b_ev = next(e for e in myev if e["kind"] == "B")
                tensor.wait_ge(S("p2te"), b_ev["n"] + 1)
                for g in range(6):
                    gi = cnt["st"] + g
                    tensor.wait_ge(S("p2xd"), gi + 1)
                    if gi >= 1:
                        tensor.wait_ge(S("p2fold"), gi)
                    tensor.ldweights(B_tm[:])
                    for q in range(8):
                        h = g * 8 + q
                        inst = tensor.matmul(pSA[:, q * 64:(q + 1) * 64],
                                             B_tm[:],
                                             Xd[:, h * 64:(h + 1) * 64],
                                             start=True, stop=True)
                    inst.then_inc(S("p2st"), 1)
            if not real:
                emit_state_groups()
            if real:
                for g in range(12):
                    gi = cnt["ymm"] + g
                    tensor.wait_ge(S("p2mt"), gi + 1)
                    tensor.wait_ge(S("p2sbf"), cnt["sbf"] + 1)
                    if gi >= 2:
                        tensor.wait_ge(S("p2ye"), gi - 1)
                    for q in range(4):
                        h = g * 4 + q
                        tensor.ldweights(MT[g][:, q * 128:(q + 1) * 128])
                        tensor.matmul(pY[gi % 2][:, q * 64:(q + 1) * 64],
                                      MT[g][:, q * 128:(q + 1) * 128],
                                      X_tm[:, h * 64:(h + 1) * 64],
                                      start=True, stop=True)
                    tensor.ldweights(C_fm[:, cc * 128:(cc + 1) * 128])
                    for q in range(4):
                        h = g * 4 + q
                        inst = tensor.matmul(
                            pY[gi % 2][:, 256 + q * 64:256 + (q + 1) * 64],
                            C_fm[:, cc * 128:(cc + 1) * 128],
                            S_bf[:, h * 64:(h + 1) * 64],
                            start=True, stop=True)
                    inst.then_inc(S("p2ymm"), 1)
                emit_state_groups()
                for e in myev:
                    if e["kind"] != "pout":
                        continue
                    tensor.wait_ge(S("p2f4"), cnt["fin"] + 1)
                    if e["pt"] >= 2:
                        tensor.wait_ge(S("p2te"), pt_to_n[e["pt"] - 2] + 1)
                    for q in range(4):
                        fs = e["i"] * 4 + q
                        inst = tensor.transpose(
                            pT[e["pt"] % 2][:, q * 128:(q + 1) * 128],
                            ybf[:, fs * 128:(fs + 1) * 128], identb[:])
                    inst.then_inc(S("p2tp"), 1)

    @block.vector
    def _(vector):
        vector.wait_ge(PH, ph0)
        vector.memset(S_f32[:], 0.0)
        vector.memset(srows[:, 1:2], EPS)
        for cc in CHS:
            real = cc >= LCH
            cnt = p2_counts(cc)
            myev = ev_by_cc[cc]
            for e in myev:
                if e["kind"] == "pout":
                    continue
                vector.wait_ge(S("p2tp"), e["n"] + 1)
                if e["kind"] == "pin":
                    if cc > 0:
                        if e["i"] == 0:
                            vector.wait_ge(S("p2ymm"), cnt["ymm"])
                    vector.tensor_copy(
                        X_tm[:, e["i"] * 512:(e["i"] + 1) * 512],
                        pT[e["pt"] % 2][:]).then_inc(S("p2te"), 1)
                elif e["kind"] == "s":
                    vector.tensor_copy(s_tm[:], pG[:, 0:48]).then_inc(S("p2te"), 1)
                elif e["kind"] == "B":
                    if cc > 0:
                        vector.wait_ge(S("p2st"), cnt["st"])
                    vector.tensor_copy(B_tm[:], pT[e["pt"] % 2][:, 0:128]
                                       ).then_inc(S("p2te"), 1)
            napre = zeros48[:] if cc == 0 else nase[:, (cc - 1) * 48:cc * 48]
            nacur = nase[:, cc * 48:(cc + 1) * 48]
            vector.tensor_tensor(t_sn[:], s_tm[:], nabc[:], ALU.mult)
            vector.tensor_tensor(dtmp[:, 0:48], t_sn[:], nacur, ALU.subtract)
            vector.tensor_tensor(dtmp[:, 48:96], napre, t_sn[:], ALU.subtract)
            vector.tensor_tensor(dtmp[:, 96:144], napre, nacur,
                                 ALU.subtract).then_inc(S("p2v1"), 1)
            if cc == LCH:
                vector.wait_ge(S("p2base"), 16)
                vector.tensor_tensor(S_f32[:], S_f32[:], y_tm[:], ALU.add)
            if real:
                vector.tensor_copy(S_bf[:], S_f32[:]).then_inc(S("p2sbf"), 1)
            vector.wait_ge(S("p2s1"), cc + 1)
            for g in range(6):
                if cc > 0:
                    vector.wait_ge(S("p2st"), 6 * (cc - 1) + g + 1)
                for q in range(8):
                    h = g * 8 + q
                    inst = vector.tensor_scalar(Xd[:, h * 64:(h + 1) * 64],
                                                X_tm[:, h * 64:(h + 1) * 64],
                                                dcol[:, h:h + 1], None, ALU.mult)
                inst.then_inc(S("p2xd"), 1)
            if real:
                if cc > LCH:
                    vector.wait_ge(S("p2f1"), cnt["fin"])

                def v_sbm(g):
                    gi = cnt["sb"] + g
                    vector.wait_ge(S("p2sb"), gi + 1)
                    if gi >= 2:
                        vector.wait_ge(S("p2exp"), gi - 1)
                    vector.tensor_tensor(sbm[gi % 2][:], pSB[gi % 2][:],
                                         mask4[:], ALU.add).then_inc(S("p2sbm"), 1)
                    for q in range(4):
                        h = g * 4 + q
                        inst = vector.tensor_scalar(
                            aD[gi % 2][:, q * 128:(q + 1) * 128],
                            sbm[gi % 2][:, q * 128:(q + 1) * 128],
                            s_tm[:, h:h + 1], abc[:, h:h + 1],
                            ALU.subtract, ALU.mult)
                    inst.then_inc(S("p2sbm"), 1)

                def v_mt(g):
                    gi = cnt["ymm"] + g
                    vector.wait_ge(S("p2exp"), gi + 1)
                    vector.wait_ge(S("p2g"), cnt["g"] + 1)
                    if gi >= 12:
                        vector.wait_ge(S("p2ymm"), gi - 11)
                    for q in range(4):
                        inst = vector.tensor_tensor(
                            MT[g][:, q * 128:(q + 1) * 128],
                            lbuf[gi % 2][:, q * 128:(q + 1) * 128],
                            pG[:], ALU.mult)
                    inst.then_inc(S("p2mt"), 1)

                def v_yasm(g):
                    gi = cnt["ymm"] + g
                    vector.wait_ge(S("p2sc1"), gi + 1)
                    vector.wait_ge(S("p2ymm"), gi + 1)
                    for q in range(4):
                        h = g * 4 + q
                        vector.scalar_tensor_tensor(
                            t4[:, q * 64:(q + 1) * 64],
                            X_tm[:, h * 64:(h + 1) * 64], dbcs[:, h:h + 1],
                            pY[gi % 2][:, q * 64:(q + 1) * 64],
                            ALU.mult, ALU.add)
                    vector.tensor_tensor(y_tm[:, g * 256:(g + 1) * 256],
                                         t4[:], yint[gi % 2][:],
                                         ALU.add).then_inc(S("p2ye"), 1)

                for g in range(12):
                    v_sbm(g)
                    if g >= 1:
                        v_mt(g - 1)
                v_mt(11)
                for g in range(12):
                    v_yasm(g)
            for g in range(6):
                gi = cnt["st"] + g
                vector.wait_ge(S("p2st"), gi + 1)
                for q in range(8):
                    h = g * 8 + q
                    vector.tensor_scalar(S_tmp[:, q * 64:(q + 1) * 64],
                                         S_f32[:, h * 64:(h + 1) * 64],
                                         dfold[:, h:h + 1], None, ALU.mult)
                for q in range(8):
                    h = g * 8 + q
                    inst = vector.tensor_tensor(S_f32[:, h * 64:(h + 1) * 64],
                                                S_tmp[:, q * 64:(q + 1) * 64],
                                                pSA[:, q * 64:(q + 1) * 64],
                                                ALU.add)
                inst.then_inc(S("p2fold"), 1)
            if real:
                vector.wait_ge(S("p2f3"), cnt["fin"] + 1)
                vector.tensor_scalar(ybf[:], y_tm[:], srows[:, 3:4], None,
                                     ALU.mult).then_inc(S("p2f4"), 1)
                for e in myev:
                    if e["kind"] != "pout":
                        continue
                    vector.wait_ge(S("p2tp"), e["n"] + 1)
                    if cc >= LCH + 2 and e["i"] == 0:
                        vector.wait_ge(S("p2yr"), 16 * (cnt["yr"] - 1))
                    vector.tensor_tensor(
                        ygc[cc % 2][:, e["i"] * 512:(e["i"] + 1) * 512],
                        pT[e["pt"] % 2][:],
                        szc[cc % 2][:, e["i"] * 512:(e["i"] + 1) * 512],
                        ALU.mult).then_inc(S("p2te"), 1)

    @block.scalar
    def _(scalar):
        scalar.wait_ge(PH, ph0)
        for cc in CHS:
            real = cc >= LCH
            cnt = p2_counts(cc)
            scalar.wait_ge(S("p2v1"), cc + 1)
            scalar.activation(dcol[:], dtmp[:, 0:48], AF.Exp)
            scalar.activation(eas[:], dtmp[:, 48:96], AF.Exp)
            scalar.activation(dfold[:], dtmp[:, 96:144],
                              AF.Exp).then_inc(S("p2s1"), 1)
            if real:
                def s_exp(g):
                    gi = cnt["sb"] + g
                    scalar.wait_ge(S("p2sbm"), 2 * (gi + 1))
                    if gi >= 2:
                        scalar.wait_ge(S("p2mt"), gi - 1)
                    scalar.activation(lbuf[gi % 2][:], aD[gi % 2][:],
                                      AF.Exp).then_inc(S("p2exp"), 1)

                def s_yint(g):
                    gi = cnt["ymm"] + g
                    scalar.wait_ge(S("p2ymm"), gi + 1)
                    if gi >= 2:
                        scalar.wait_ge(S("p2ye"), gi - 1)
                    for q in range(4):
                        h = g * 4 + q
                        inst = scalar.activation(
                            yint[gi % 2][:, q * 64:(q + 1) * 64],
                            pY[gi % 2][:, 256 + q * 64:256 + (q + 1) * 64],
                            AF.Copy, scale=eas[:, h:h + 1])
                    inst.then_inc(S("p2sc1"), 1)

                for g in range(12):
                    s_exp(g)
                for g in range(12):
                    s_yint(g)
                scalar.wait_ge(S("p2ye"), cnt["ymm"] + 12)
                scalar.memzero(srows[:, 0:1])
                scalar.activation(ysq[:], y_tm[:], AF.Square,
                                  accum_out=srows[:, 0:1]).then_inc(S("p2f1"), 1)
                scalar.activation(srows[:, 2:3], srows[:, 0:1], AF.Ln,
                                  scale=1.0 / DIN, bias=srows[:, 1:2])
                scalar.activation(srows[:, 3:4], srows[:, 2:3], AF.Exp,
                                  scale=-0.5).then_inc(S("p2f3"), 1)

    @block.gpsimd
    def _(gpsimd):
        gpsimd.wait_ge(PH, ph0)
        for cc in range(LCH, stop):
            rc = cc - LCH
            done_te = [e["n"] + 1 for e in ev_by_cc[cc] if e["kind"] == "pout"][-1]
            gpsimd.wait_ge(S("p2te"), done_te)
            rt0 = rc * 128
            gpsimd.dma_start(
                out=YR[:, :, rt0:rt0 + 128].rearrange("c p t -> p c t"),
                in_=ygc[cc % 2][:].rearrange("p (c t) -> p c t", t=128)
            ).then_inc(S("p2yr"), 16)

    NREAL = stop - LCH
    JUNK = env["JUNK"]

    @block.sync
    def _(sync):
        sync.wait_ge(S("p2yr"), 16 * NREAL)
        sync.dma_start(out=JUNK[0:1, 2:3], in_=S_fm[0:1, 0:1]).then_inc(PH, 16)

    @block.gpsimd
    def _(gpsimd):
        gpsimd.wait_ge(S("p2yr"), 16 * NREAL)
        gpsimd.dma_start(out=JUNK[0:1, 3:4], in_=S_fm[0:1, 0:1]).then_inc(PH, 16)

    @block.vector
    def _(vector):
        vector.wait_ge(S("p2fold"), 6 * stop)
        vector.wait_ge(S("p2te"), NEV)
        vector.engine_nop().then_inc(PH, 1)

    @block.scalar
    def _(scalar):
        scalar.wait_ge(S("p2f3"), NREAL)
        scalar.activation(srows[:, 2:3], srows[:, 2:3],
                          AF.Copy).then_inc(PH, 1)

    @block.tensor
    def _(tensor):
        tensor.wait_ge(S("p2te"), NEV)
        tensor.transpose(pG[:, 0:48], S_fm[:, 0:128],
                         identf[0:48, 0:48]).then_inc(PH, 1)

    if _P2_KEEP:
        env["p2bufs"] = dict(X_tm=X_tm, Xd=Xd, B_tm=B_tm, s_tm=s_tm,
                             t_sn=t_sn, dcol=dcol, eas=eas, dfold=dfold,
                             S_f32=S_f32, S_bf=S_bf, sbm0=sbm[0], aD0=aD[0],
                             lb0=lbuf[0], MT0=MT[0], MT11=MT[11],
                             y_tm=y_tm, ybf=ybf, srows=srows, srow=srow)
    else:
        p2.close()
    return ph0 + 16 + 16 + 1 + 1 + 1


def emit_p3(nc, env, ph0):
    block = env["block"]
    S = env["S"]
    PH = S("ph")
    XT, WOUT, WG9 = env["XT"], env["WOUT"], env["WG9"]
    YR, X2, X2N, W9D = env["YR"], env["X2"], env["X2N"], env["W9D"]
    innerw, onerow, onecolb, identf = (env["innerw"], env["onerow"],
                                      env["onecolb"], env["identf"])
    idx8s = env["idx8s"]
    mask4 = env["mask4"]

    p3 = ExitStack()
    sbt = lambda n, shp, dt=F32: p3.enter_context(nc.sbuf_tensor(n, shp, dt))
    pst = lambda n, shp, dt=F32: p3.enter_context(nc.psum_tensor(n, shp, dt))
    wsb = sbt("p3_w", [128, NCC * DM], BF16)
    wg9 = sbt("p3_wg9", [128, NKD * 9])
    yrt = [sbt(f"p3_yr{i}", [128, NCC * 256], BF16) for i in range(2)]
    xt = sbt("p3_x", [128, NKD * 256])
    sqb = sbt("p3_sqb", [128, NKD * 256], BF16)
    cbb = sbt("p3_cbb", [128, NKD * 256], BF16)
    ao = sbt("p3_ao", [128, NKD * 256])
    x2f = sbt("p3_x2f", [128, NKD * 256])
    x2nb = sbt("p3_x2nb", [128, NKD * 256], BF16)
    x2nf = sbt("p3_x2nf", [128, 256])
    rows = sbt("p3_rows", [1, 256 * 4])
    lg = sbt("p3_lg", [9, 256])
    ltm = sbt("p3_ltm", [128, 18])
    t8a = sbt("p3_t8a", [128, 16])
    t8b = sbt("p3_t8b", [128, 16])
    t8c = sbt("p3_t8c", [128, 16])
    t8d = sbt("p3_t8d", [128, 16])
    w9tm = sbt("p3_w9tm", [128, 18])
    rc = sbt("p3_rc", [128, 32])
    dmy = sbt("p3_dmy", [128, 256])
    w9r = sbt("p3_w9r", [9, 256])
    ps_sx = pst("p3_psx", [1, 256])
    ps_sq = pst("p3_psq", [1, 256])
    ps_m = pst("p3_pm", [128, 256])
    ps_r = pst("p3_pr", [128, 256])
    pp = [pst(f"p3_pp{i}", [128, 256]) for i in range(2)]
    plg = pst("p3_plg", [9, 256])
    ptr = pst("p3_ptr", [128, 9])

    RM, RV, RT, RR = range(4)

    def row(i):
        return rows[0:1, i * 256:(i + 1) * 256]

    NT3 = NTOK // 256

    @block.gpsimd
    def _(gpsimd):
        gpsimd.wait_ge(PH, ph0)
        for k in range(NCC):
            gpsimd.dma_start(out=wsb[:, k * DM:(k + 1) * DM],
                             in_=WOUT[k * 128:(k + 1) * 128, :]
                             ).then_inc(S("p3w"), 16)
        gpsimd.dma_start(out=wg9[:].rearrange("p (k n) -> p k n", n=9),
                         in_=WG9.rearrange("(k p) n -> p k n", p=128)
                         ).then_inc(S("p3w"), 16)
        for t in range(NT3):
            gpsimd.wait_ge(S("p3w9"), 2 * (t + 1))
            gpsimd.dma_start(out=W9D[:, t * 256:(t + 1) * 256],
                             in_=w9r[:]).then_inc(S("p3ws9"), 16)

    @block.sync
    def _(sync):
        sync.wait_ge(PH, ph0)
        for t in range(NT3):
            if t >= 2:
                sync.wait_ge(S("p3mm"), 16 * (t - 1))
            if t >= 1:
                sync.wait_ge(S("p3x2"), 16 * t)
            src = XT.rearrange("(k p) t -> p k t", p=128)
            sync.dma_start(out=xt[:].rearrange("p (k t) -> p k t", t=256),
                           in_=src[:, :, LEAD + t * 256:LEAD + (t + 1) * 256]
                           ).then_inc(S("p3x"), 16)
            sync.dma_start(
                out=yrt[t % 2][:].rearrange("p (c t) -> p c t", t=256),
                in_=YR[:, :, t * 256:(t + 1) * 256].rearrange("c p t -> p c t")
            ).then_inc(S("p3x"), 16)
            sync.wait_ge(S("p3x2"), 16 * (t + 1))
            sync.dma_start(
                out=X2[:, :, t * 256:(t + 1) * 256].rearrange("c p t -> p c t"),
                in_=x2f[:].rearrange("p (c t) -> p c t", t=256)
            ).then_inc(S("p3so"), 16)
            sync.wait_ge(S("p3xn"), 16 * (t + 1))
            sync.dma_start(
                out=X2N[:, :, t * 256:(t + 1) * 256].rearrange("c p t -> p c t"),
                in_=x2nb[:].rearrange("p (c t) -> p c t", t=256)
            ).then_inc(S("p3so"), 16)

    @block.scalar
    def _(scalar):
        scalar.wait_ge(PH, ph0)
        for t in range(NT3):
            for rnd in range(3):
                ri = 3 * t + rnd
                src = [xt, ao, x2f][rnd]
                if rnd == 0:
                    scalar.wait_ge(S("p3x"), 32 * t + 16)
                elif rnd == 1:
                    scalar.wait_ge(S("p3ao"), 16 * (t + 1))
                else:
                    scalar.wait_ge(S("p3x2"), 16 * (t + 1))
                if ri > 0:
                    scalar.wait_ge(S("p3st"), ri)
                scalar.activation(sqb[:], src[:], AF.Square).then_inc(S("p3sq"), 1)
                scalar.wait_ge(S("p3rv"), ri + 1)
                scalar.activation(row(RT), row(RV), AF.Ln)
                scalar.activation(row(RR), row(RT), AF.Exp,
                                  scale=-0.5).then_inc(S("p3rs"), 1)
            scalar.wait_ge(S("p3ra"), 4 * t + 1)
            for g in range(2):
                scalar.activation(t8b[:, g * 8:g * 8 + 8],
                                  t8a[:, g * 8:g * 8 + 8], AF.Exp)
                scalar.activation(rc[:, g * 16 + 10:g * 16 + 11],
                                  rc[:, g * 16 + 9:g * 16 + 10], AF.Exp,
                                  scale=-1.0)
                scalar.activation(rc[:, g * 16 + 11:g * 16 + 12],
                                  rc[:, g * 16 + 10:g * 16 + 11], AF.Ln,
                                  bias=1.0)
            scalar.activation(rc[:, 11:12], rc[:, 11:12],
                              AF.Copy).then_inc(S("p3rb"), 1)
            scalar.wait_ge(S("p3ra"), 4 * t + 2)
            for g in range(2):
                scalar.activation(rc[:, g * 16 + 2:g * 16 + 3],
                                  rc[:, g * 16 + 1:g * 16 + 2], AF.Ln)
                scalar.activation(rc[:, g * 16 + 13:g * 16 + 14],
                                  rc[:, g * 16 + 12:g * 16 + 13], AF.Exp,
                                  scale=-1.0)
            scalar.activation(rc[:, 13:14], rc[:, 13:14],
                              AF.Copy).then_inc(S("p3rb"), 1)
            scalar.wait_ge(S("p3ra"), 4 * t + 3)
            for g in range(2):
                scalar.activation(t8b[:, g * 8:g * 8 + 8],
                                  t8d[:, g * 8:g * 8 + 8], AF.Exp)
            scalar.activation(t8b[:, 0:1], t8b[:, 0:1],
                              AF.Copy).then_inc(S("p3rb"), 1)

    @block.vector
    def _(vector):
        vector.wait_ge(PH, ph0)
        vector.wait_ge(S("p3w"), 16 * (NCC + 1))
        for k in range(NCC):
            inst = vector.tensor_scalar(wsb[:, k * DM:(k + 1) * DM],
                                        wsb[:, k * DM:(k + 1) * DM],
                                        innerw[:, k:k + 1], None, ALU.mult)
        inst.then_inc(S("p3ws"), 1)
        for t in range(NT3):
            for rnd in range(3):
                ri = 3 * t + rnd
                src = [xt, ao, x2f][rnd]
                if rnd == 0:
                    vector.wait_ge(S("p3x"), 32 * t + 16)
                if ri > 0:
                    vector.wait_ge(S("p3st"), ri)
                vector.tensor_copy(cbb[:], src[:]).then_inc(S("p3cb"), 1)
                vector.wait_ge(S("p3st"), ri + 1)
                vector.tensor_scalar(row(RM), ps_sx[0:1, :], 1.0 / DM, None,
                                     ALU.mult)
                vector.tensor_scalar(row(RV), ps_sq[0:1, :], 1.0 / DM, None,
                                     ALU.mult)
                vector.tensor_tensor(row(RT), row(RM), row(RM), ALU.mult)
                vector.tensor_tensor(row(RV), row(RV), row(RT), ALU.subtract)
                vector.tensor_scalar(row(RV), row(RV), EPS, None,
                                     ALU.add).then_inc(S("p3rv"), 1)
                vector.wait_ge(S("p3bc"), ri + 1)
                if rnd == 0:
                    for o in range(NKD):
                        vector.wait_ge(S("p3mm"), 16 * t + o + 1)
                        osl = slice(o * 256, (o + 1) * 256)
                        vector.tensor_tensor(ao[:, osl], xt[:, osl], ps_m[:],
                                             ALU.subtract)
                        vector.tensor_tensor(ao[:, osl], ao[:, osl], ps_r[:],
                                             ALU.mult)
                        vector.tensor_tensor(ao[:, osl], ao[:, osl],
                                             pp[o % 2][:],
                                             ALU.add).then_inc(S("p3ao"), 1)
                elif rnd == 1:
                    if t > 0:
                        vector.wait_ge(S("p3so"), 32 * t - 16)
                    for o in range(NKD):
                        osl = slice(o * 256, (o + 1) * 256)
                        vector.tensor_tensor(x2f[:, osl], ao[:, osl], ps_m[:],
                                             ALU.subtract)
                        vector.tensor_tensor(x2f[:, osl], x2f[:, osl], ps_r[:],
                                             ALU.mult)
                        vector.tensor_tensor(x2f[:, osl], x2f[:, osl],
                                             xt[:, osl],
                                             ALU.add).then_inc(S("p3x2"), 1)
                else:
                    if t > 0:
                        vector.wait_ge(S("p3so"), 32 * t)
                    for o in range(NKD):
                        osl = slice(o * 256, (o + 1) * 256)
                        if o > 0 or t > 0:
                            vector.wait_ge(S("p3lg"), 16 * t + o)
                        vector.tensor_tensor(x2nf[:], x2f[:, osl], ps_m[:],
                                             ALU.subtract)
                        vector.tensor_tensor(x2nf[:], x2nf[:], ps_r[:],
                                             ALU.mult).then_inc(S("p3xnf"), 1)
                        vector.tensor_copy(x2nb[:, osl],
                                           x2nf[:]).then_inc(S("p3xn"), 1)
            vector.wait_ge(S("p3lg"), 16 * (t + 1))
            vector.tensor_copy(lg[:], plg[:]).then_inc(S("p3lgv"), 1)
            def DMYOP():
                vector.tensor_copy(dmy[:], mask4[:, 0:256])

            def lt(g):
                return ltm[:, g * 9:g * 9 + 9]

            def a8(buf, g):
                return buf[:, g * 8:g * 8 + 8]

            def c1(i, g):
                return rc[:, g * 16 + i:g * 16 + i + 1]

            for g in range(2):
                gi = 2 * t + g
                vector.wait_ge(S("p3rt"), gi + 1)
                vector.tensor_copy(lt(g), ptr[:]).then_inc(S("p3trv"), 1)
            DMYOP()
            for g in range(2):
                vector.tensor_reduce(c1(0, g), lt(g)[:, 0:8], axis=AX.X,
                                     op=ALU.max)
            DMYOP()
            for g in range(2):
                vector.tensor_scalar(a8(t8a, g), lt(g)[:, 0:8], c1(0, g),
                                     None, ALU.subtract)
                vector.tensor_scalar(c1(7, g), lt(g)[:, 8:9], -1.0, None,
                                     ALU.mult)
            DMYOP()
            inst = None
            for g in range(2):
                vector.tensor_relu(c1(8, g), c1(7, g))
                inst = vector.tensor_tensor(c1(9, g), c1(7, g), lt(g)[:, 8:9],
                                            ALU.max)
            inst.then_inc(S("p3ra"), 1)
            vector.wait_ge(S("p3rb"), 3 * t + 1)
            for g in range(2):
                vector.tensor_reduce(c1(1, g), a8(t8b, g), axis=AX.X,
                                     op=ALU.add)
                inst = vector.tensor_tensor(c1(12, g), c1(8, g), c1(11, g),
                                            ALU.add)
            inst.then_inc(S("p3ra"), 1)
            for g in range(2):
                vector.tensor_reduce(c1(3, g), a8(t8a, g), axis=AX.X,
                                     op=ALU.max)
            DMYOP()
            for g in range(2):
                vector.tensor_scalar(a8(t8c, g), a8(t8a, g), c1(3, g), None,
                                     ALU.is_ge)
            DMYOP()
            for g in range(2):
                vector.tensor_tensor(a8(t8b, g), a8(t8c, g), idx8s[:],
                                     ALU.mult)
                vector.tensor_scalar(a8(t8d, g), a8(t8c, g), -BIG, BIG,
                                     ALU.mult, ALU.add)
            DMYOP()
            for g in range(2):
                vector.tensor_tensor(a8(t8b, g), a8(t8b, g), a8(t8d, g),
                                     ALU.add)
            DMYOP()
            for g in range(2):
                vector.tensor_reduce(c1(4, g), a8(t8b, g), axis=AX.X,
                                     op=ALU.min)
            DMYOP()
            for g in range(2):
                vector.tensor_scalar(a8(t8c, g), idx8s[:], c1(4, g), None,
                                     ALU.is_equal)
            DMYOP()
            for g in range(2):
                vector.scalar_tensor_tensor(a8(t8b, g), a8(t8c, g), -BIG,
                                            a8(t8a, g), ALU.mult, ALU.add)
            DMYOP()
            for g in range(2):
                vector.tensor_reduce(c1(5, g), a8(t8b, g), axis=AX.X,
                                     op=ALU.max)
            DMYOP()
            for g in range(2):
                vector.tensor_scalar(a8(t8d, g), a8(t8b, g), c1(5, g), None,
                                     ALU.is_ge)
            DMYOP()
            for g in range(2):
                vector.tensor_tensor(a8(t8b, g), a8(t8d, g), idx8s[:],
                                     ALU.mult)
                vector.tensor_scalar(lt(g)[:, 0:8], a8(t8d, g), -BIG, BIG,
                                     ALU.mult, ALU.add)
            DMYOP()
            for g in range(2):
                vector.tensor_tensor(a8(t8b, g), a8(t8b, g), lt(g)[:, 0:8],
                                     ALU.add)
            DMYOP()
            for g in range(2):
                vector.tensor_reduce(c1(6, g), a8(t8b, g), axis=AX.X,
                                     op=ALU.min)
            DMYOP()
            inst = None
            for g in range(2):
                inst = vector.tensor_scalar(a8(t8d, g), a8(t8a, g), c1(2, g),
                                            None, ALU.subtract)
            inst.then_inc(S("p3ra"), 1)
            for g in range(2):
                vector.tensor_scalar(a8(t8a, g), idx8s[:], c1(6, g), None,
                                     ALU.is_equal)
            DMYOP()
            for g in range(2):
                vector.tensor_tensor(a8(t8c, g), a8(t8c, g), a8(t8a, g),
                                     ALU.add)
            vector.wait_ge(S("p3rb"), 3 * t + 3)
            DMYOP()
            inst = None
            for g in range(2):
                vector.tensor_tensor(w9tm[:, g * 9:g * 9 + 8], a8(t8b, g),
                                     a8(t8c, g), ALU.mult)
                inst = vector.tensor_copy(w9tm[:, g * 9 + 8:g * 9 + 9],
                                          c1(13, g))
            inst.then_inc(S("p3ra"), 1)
            for g in range(2):
                gi = 2 * t + g
                vector.wait_ge(S("p3rt2"), gi + 1)
                if t > 0 and g == 0:
                    vector.wait_ge(S("p3ws9"), 16 * t)
                vector.tensor_copy(w9r[:, g * 128:(g + 1) * 128],
                                   plg[:, g * 128:(g + 1) * 128]
                                   ).then_inc(S("p3w9"), 1)

    @block.tensor
    def _(tensor):
        tensor.wait_ge(PH, ph0)
        tensor.wait_ge(S("p3ws"), 1)
        for t in range(NT3):
            for rnd in range(3):
                ri = 3 * t + rnd
                tensor.wait_ge(S("p3sq"), ri + 1)
                tensor.wait_ge(S("p3cb"), ri + 1)
                if ri > 0:
                    tensor.wait_ge(S("p3rv"), ri)
                for k in range(NKD):
                    tensor.matmul(ps_sx[:], onecolb[:],
                                  cbb[:, k * 256:(k + 1) * 256],
                                  start=(k == 0), stop=(k == NKD - 1))
                for k in range(NKD):
                    inst = tensor.matmul(ps_sq[:], onecolb[:],
                                         sqb[:, k * 256:(k + 1) * 256],
                                         start=(k == 0), stop=(k == NKD - 1))
                inst.then_inc(S("p3st"), 1)
                tensor.wait_ge(S("p3rs"), ri + 1)
                if ri > 0:
                    prnd = (ri - 1) % 3
                    pt = (ri - 1) // 3
                    if prnd == 0:
                        tensor.wait_ge(S("p3ao"), 16 * (pt + 1))
                    elif prnd == 1:
                        tensor.wait_ge(S("p3x2"), 16 * (pt + 1))
                    else:
                        tensor.wait_ge(S("p3xn"), 16 * (pt + 1))
                tensor.matmul(ps_m[:], onerow[:], row(RM), start=True, stop=True)
                tensor.matmul(ps_r[:], onerow[:], row(RR),
                              start=True, stop=True).then_inc(S("p3bc"), 1)
                if rnd == 0:
                    tensor.wait_ge(S("p3x"), 32 * (t + 1))
                    for o in range(NKD):
                        if 16 * t + o >= 2:
                            tensor.wait_ge(S("p3ao"), 16 * t + o - 1)
                        for k in range(NCC):
                            inst = tensor.matmul(
                                pp[o % 2][:],
                                wsb[:, k * DM + o * 128:k * DM + (o + 1) * 128],
                                yrt[t % 2][:, k * 256:(k + 1) * 256],
                                start=(k == 0), stop=(k == NCC - 1))
                        inst.then_inc(S("p3mm"), 1)
                elif rnd == 2:
                    if t > 0:
                        tensor.wait_ge(S("p3lgv"), t)
                        tensor.wait_ge(S("p3w9"), 2 * t)
                    for o in range(NKD):
                        tensor.wait_ge(S("p3xnf"), 16 * t + o + 1)
                        inst = tensor.matmul(plg[:], wg9[:, o * 9:(o + 1) * 9],
                                             x2nf[:], start=(o == 0),
                                             stop=(o == NKD - 1))
                        inst.then_inc(S("p3lg"), 1)
            tensor.wait_ge(S("p3lgv"), t + 1)
            for grp in range(2):
                gi = 2 * t + grp
                if gi >= 1:
                    tensor.wait_ge(S("p3trv"), gi)
                tensor.transpose(ptr[:], lg[:, grp * 128:(grp + 1) * 128],
                                 identf[0:9, 0:9]).then_inc(S("p3rt"), 1)
            tensor.wait_ge(S("p3ra"), 4 * t + 4)
            for grp in range(2):
                tensor.transpose(plg[:, grp * 128:(grp + 1) * 128],
                                 w9tm[:, grp * 9:grp * 9 + 9],
                                 identf[:]).then_inc(S("p3rt2"), 1)

    JUNK = env["JUNK"]

    @block.sync
    def _(sync):
        sync.wait_ge(S("p3so"), 32 * NT3)
        sync.dma_start(out=JUNK[0:1, 4:5], in_=rows[0:1, 0:1]).then_inc(PH, 16)

    @block.gpsimd
    def _(gpsimd):
        gpsimd.wait_ge(S("p3ws9"), 16 * NT3)
        gpsimd.dma_start(out=JUNK[0:1, 5:6], in_=rows[0:1, 0:1]).then_inc(PH, 16)

    @block.vector
    def _(vector):
        vector.wait_ge(S("p3w9"), 2 * NT3)
        vector.engine_nop().then_inc(PH, 1)

    @block.scalar
    def _(scalar):
        scalar.wait_ge(S("p3rs"), 3 * NT3)
        scalar.wait_ge(S("p3rb"), 3 * NT3)
        scalar.activation(rc[:, 0:1], rc[:, 0:1], AF.Copy).then_inc(PH, 1)

    @block.tensor
    def _(tensor):
        tensor.wait_ge(S("p3rt2"), 2 * NT3)
        tensor.transpose(ptr[:], lg[:, 0:128],
                         identf[0:9, 0:9]).then_inc(PH, 1)

    p3.close()
    return ph0 + 16 + 16 + 1 + 1 + 1


P4_PAIRS = []
for _s in range(9):
    npair = 44 if _s == 0 else 11
    base = 0 if _s == 0 else 88 + (_s - 1) * 22
    for _p in range(npair):
        gcol = base + _p
        ucol = base + npair + _p
        hc = _p if _s == 0 else 44 + (_s - 1) * 11 + _p
        P4_PAIRS.append(dict(s=_s, p=_p, g=gcol, u=ucol, hc=hc,
                             idx=len(P4_PAIRS)))
P4_SRC_START = {}
for _e in P4_PAIRS:
    P4_SRC_START.setdefault(_e["s"], _e["idx"])


def emit_p4(nc, env, ph0):
    block = env["block"]
    S = env["S"]
    PH = S("ph")
    X2N, WUPC, W9D, HD, JUNK = (env["X2N"], env["WUPC"], env["W9D"],
                                env["HD"], env["JUNK"])
    onerow = env["onerow"]

    p4 = ExitStack()
    sbt = lambda n, shp, dt=F32: p4.enter_context(nc.sbuf_tensor(n, shp, dt))
    pst = lambda n, shp, dt=F32: p4.enter_context(nc.psum_tensor(n, shp, dt))
    x2nA = sbt("p4_x2n", [128, NKD * NTOK], BF16)
    wg = [sbt(f"p4_wg{i}", [128, NKD * 128], BF16) for i in range(2)]
    wu = [sbt(f"p4_wu{i}", [128, NKD * 128], BF16) for i in range(2)]
    hb = [sbt(f"p4_hb{i}", [128, NTOK], BF16) for i in range(2)]
    wbc = sbt("p4_wbc", [128, NTOK], BF16)
    w1row = sbt("p4_w1r", [1, NTOK])
    t1 = sbt("p4_t1", [128, 1024], BF16)
    t2 = sbt("p4_t2", [128, 1024], BF16)
    pg = [pst(f"p4_pg{i}", [128, 1024]) for i in range(2)]
    pu = [pst(f"p4_pu{i}", [128, 1024]) for i in range(2)]

    NP = len(P4_PAIRS)

    @block.sync
    def _(sync):
        sync.wait_ge(PH, ph0)
        sync.wait_ge(S("wupc"), 16)
        for k in range(NKD):
            sync.dma_start(out=x2nA[:, k * NTOK:(k + 1) * NTOK],
                           in_=X2N[k]).then_inc(S("p4x"), 16)
        wsrc = WUPC.rearrange("(k p) n -> p k n", p=128)
        for e in P4_PAIRS:
            i = e["idx"]
            if e["p"] == 0:
                si = e["s"]
                r9 = 8 if si == 0 else si - 1
                if si > 0:
                    sync.wait_ge(S("p4wb"), 8 * si)
                sync.dma_start(out=w1row[:],
                               in_=W9D[r9:r9 + 1, :]).then_inc(S("p4wr"), 16)
            if i >= 2:
                sync.wait_ge(S("p4mm"), 8 * (i - 1))
            sync.dma_start(
                out=wg[i % 2][:].rearrange("p (k n) -> p k n", n=128),
                in_=wsrc[:, :, e["g"] * 128:(e["g"] + 1) * 128]
            ).then_inc(S("p4w"), 16)
            sync.dma_start(
                out=wu[i % 2][:].rearrange("p (k n) -> p k n", n=128),
                in_=wsrc[:, :, e["u"] * 128:(e["u"] + 1) * 128]
            ).then_inc(S("p4w"), 16)

    @block.tensor
    def _(tensor):
        tensor.wait_ge(PH, ph0)
        tensor.wait_ge(S("p4x"), 16 * NKD)
        for e in P4_PAIRS:
            i = e["idx"]
            if e["p"] == 0:
                si = e["s"]
                tensor.wait_ge(S("p4wr"), 16 * (si + 1))
                if i > 0:
                    tensor.wait_ge(S("p4sl"), 4 * i)
                    tensor.wait_ge(S("p4ev"), 4 * i)
                for b in range(8):
                    if si > 0 or b >= 2:
                        tensor.wait_ge(S("p4wbe"), 8 * si + b - 2 + 2)
                    tensor.matmul(pg[b % 2][:, 0:512], onerow[:],
                                  w1row[0:1, b * 512:(b + 1) * 512],
                                  start=True, stop=True).then_inc(S("p4wb"), 1)
                tensor.wait_ge(S("p4wbe"), 8 * (si + 1))
            tensor.wait_ge(S("p4w"), 32 * (i + 1))
            for h in range(4):
                for part in range(2):
                    wb = wg[i % 2] if part == 0 else wu[i % 2]
                    pb = pg if part == 0 else pu
                    if 4 * i + h >= 2:
                        tensor.wait_ge(S("p4sl" if part == 0 else "p4ev"),
                                       4 * i + h - 1)
                    for k in range(NKD):
                        tensor.ldweights(wb[:, k * 128:(k + 1) * 128])
                        for q in range(2):
                            t0 = h * 1024 + q * 512
                            inst = tensor.matmul(
                                pb[h % 2][:, q * 512:(q + 1) * 512],
                                wb[:, k * 128:(k + 1) * 128],
                                x2nA[:, k * NTOK + t0:k * NTOK + t0 + 512],
                                start=(k == 0), stop=(k == NKD - 1))
                    inst.then_inc(S("p4mm"), 1)

    @block.scalar
    def _(scalar):
        scalar.wait_ge(PH, ph0)
        for e in P4_PAIRS:
            i = e["idx"]
            for h in range(4):
                scalar.wait_ge(S("p4mm"), 8 * i + 2 * h + 1)
                if 4 * i + h >= 1:
                    scalar.wait_ge(S("p4ev"), 4 * i + h)
                scalar.activation(t1[:], pg[h % 2][:],
                                  AF.Silu).then_inc(S("p4sl"), 1)

    @block.vector
    def _(vector):
        vector.wait_ge(PH, ph0)
        for e in P4_PAIRS:
            i = e["idx"]
            if e["p"] == 0:
                si = e["s"]
                for b in range(8):
                    vector.wait_ge(S("p4wb"), 8 * si + b + 1)
                    vector.tensor_copy(wbc[:, b * 512:(b + 1) * 512],
                                       pg[b % 2][:, 0:512]
                                       ).then_inc(S("p4wbe"), 1)
            for h in range(4):
                hi = 4 * i + h
                vector.wait_ge(S("p4sl"), hi + 1)
                vector.wait_ge(S("p4mm"), 8 * i + 2 * h + 2)
                if i >= 2 and h == 0:
                    vector.wait_ge(S("p4hs"), 16 * (i - 1))
                vector.tensor_tensor(t2[:], t1[:], pu[h % 2][:], ALU.mult)
                vector.tensor_tensor(hb[i % 2][:, h * 1024:(h + 1) * 1024],
                                     t2[:], wbc[:, h * 1024:(h + 1) * 1024],
                                     ALU.mult).then_inc(S("p4ev"), 1)

    @block.gpsimd
    def _(gpsimd):
        gpsimd.wait_ge(PH, ph0)
        for e in P4_PAIRS:
            i = e["idx"]
            gpsimd.wait_ge(S("p4ev"), 4 * (i + 1))
            gpsimd.dma_start(out=HD[e["hc"]],
                             in_=hb[i % 2][:]).then_inc(S("p4hs"), 16)

    @block.sync
    def _(sync):
        sync.wait_ge(S("p4hs"), 16 * NP)
        sync.dma_start(out=JUNK[0:1, 6:7], in_=w1row[0:1, 0:1]).then_inc(PH, 16)

    @block.gpsimd
    def _(gpsimd):
        gpsimd.wait_ge(S("p4hs"), 16 * NP)
        gpsimd.dma_start(out=JUNK[0:1, 7:8], in_=w1row[0:1, 0:1]).then_inc(PH, 16)

    @block.vector
    def _(vector):
        vector.wait_ge(S("p4ev"), 4 * NP)
        vector.engine_nop().then_inc(PH, 1)

    @block.scalar
    def _(scalar):
        scalar.wait_ge(S("p4sl"), 4 * NP)
        scalar.activation(t1[:, 0:1], t1[:, 0:1], AF.Copy).then_inc(PH, 1)

    @block.tensor
    def _(tensor):
        tensor.wait_ge(S("p4mm"), 8 * NP)
        tensor.matmul(pg[0][:, 0:512], onerow[:], w1row[0:1, 0:512],
                      start=True, stop=True).then_inc(PH, 1)

    p4.close()
    return ph0 + 16 + 16 + 1 + 1 + 1


def emit_p5(nc, env, ph0):
    block = env["block"]
    S = env["S"]
    PH = S("ph")
    X2, W2C, HD, OUT = env["X2"], env["W2C"], env["HD"], env["OUT"]
    onerow, onecolb = env["onerow"], env["onecolb"]
    eps1 = env["dtb"]

    p5 = ExitStack()
    sbt = lambda n, shp, dt=F32: p5.enter_context(nc.sbuf_tensor(n, shp, dt))
    pst = lambda n, shp, dt=F32: p5.enter_context(nc.psum_tensor(n, shp, dt))
    wk = [sbt(f"p5_wk{i}", [128, 4 * 1024], BF16) for i in range(2)]
    hk = [sbt(f"p5_hk{i}", [128, 4 * 512], BF16) for i in range(2)]
    ff = sbt("p5_ff", [128, NKD * 512])
    cb = sbt("p5_cb", [128, NKD * 512], BF16)
    sq = sbt("p5_sq", [128, NKD * 512], BF16)
    x2t = sbt("p5_x2", [128, NKD * 512])
    obuf = sbt("p5_ob", [128, NKD * 512])
    rows = sbt("p5_rows", [1, 512 * 4])
    po = [pst(f"p5_po{i}", [128, 512]) for i in range(8)]

    RM, RV, RT, RR = range(4)

    def row(i):
        return rows[0:1, i * 512:(i + 1) * 512]

    NT5 = NTOK // 512
    NKB = NHC // 4

    @block.sync
    def _(sync):
        sync.wait_ge(PH, ph0)
        sync.wait_ge(S("w2c"), 16)
        wsrc = W2C.rearrange("(c p) n -> p c n", p=128)
        for ts in range(NT5):
            for hf in range(2):
                for kb in range(NKB):
                    gkb = (2 * ts + hf) * NKB + kb
                    if gkb >= 2:
                        sync.wait_ge(S("p5mm"), gkb - 1)
                    sync.dma_start(
                        out=wk[gkb % 2][:].rearrange("p (c n) -> p c n",
                                                     n=1024),
                        in_=wsrc[:, 4 * kb:4 * kb + 4,
                                 hf * 1024:(hf + 1) * 1024]
                    ).then_inc(S("p5w"), 16)
            sync.wait_ge(S("p5x3"), 16 * ts)
            sync.dma_start(out=x2t[:].rearrange("p (c t) -> p c t", t=512),
                           in_=X2[:, :, ts * 512:(ts + 1) * 512]
                           .rearrange("c p t -> p c t")).then_inc(S("p5x2"), 16)

    @block.gpsimd
    def _(gpsimd):
        gpsimd.wait_ge(PH, ph0)
        for ts in range(NT5):
            for hf in range(2):
                for kb in range(NKB):
                    gkb = (2 * ts + hf) * NKB + kb
                    if gkb >= 2:
                        gpsimd.wait_ge(S("p5mm"), gkb - 1)
                    gpsimd.dma_start(
                        out=hk[gkb % 2][:].rearrange("p (c t) -> p c t",
                                                     t=512),
                        in_=HD[4 * kb:4 * kb + 4, :,
                               ts * 512:(ts + 1) * 512]
                        .rearrange("c p t -> p c t")).then_inc(S("p5h"), 16)
            gpsimd.wait_ge(S("p5x3"), 16 * (ts + 1))
            gpsimd.dma_start(out=OUT[:, :, ts * 512:(ts + 1) * 512]
                             .rearrange("c p t -> p c t"),
                             in_=obuf[:].rearrange("p (c t) -> p c t", t=512)
                             ).then_inc(S("p5out"), 16)

    @block.tensor
    def _(tensor):
        tensor.wait_ge(PH, ph0)
        for ts in range(NT5):
            for hf in range(2):
                half = 2 * ts + hf
                for kb in range(NKB):
                    gkb = half * NKB + kb
                    tensor.wait_ge(S("p5w"), 16 * (gkb + 1))
                    tensor.wait_ge(S("p5h"), 16 * (gkb + 1))
                    if kb == 0:
                        if half >= 1:
                            tensor.wait_ge(S("p5ev"), 8 * half)
                        if hf == 0 and ts >= 1:
                            tensor.wait_ge(S("p5x3"), 16 * ts)
                    for c4 in range(4):
                        kc = 4 * kb + c4
                        for o in range(8):
                            inst = tensor.matmul(
                                po[o][:],
                                wk[gkb % 2][:, c4 * 1024 + o * 128:
                                            c4 * 1024 + (o + 1) * 128],
                                hk[gkb % 2][:, c4 * 512:(c4 + 1) * 512],
                                start=(kc == 0), stop=(kc == NHC - 1))
                    inst.then_inc(S("p5mm"), 1)
            tensor.wait_ge(S("p5cb"), 16 * (ts + 1))
            for oo in range(NKD):
                tensor.matmul(po[0][0:1, :], onecolb[:],
                              cb[:, oo * 512:(oo + 1) * 512],
                              start=(oo == 0), stop=(oo == NKD - 1))
            for oo in range(NKD):
                inst = tensor.matmul(po[1][0:1, :], onecolb[:],
                                     sq[:, oo * 512:(oo + 1) * 512],
                                     start=(oo == 0), stop=(oo == NKD - 1))
            inst.then_inc(S("p5st"), 1)
            tensor.wait_ge(S("p5rs"), ts + 1)
            tensor.matmul(po[2][:], onerow[:], row(RM), start=True, stop=True)
            tensor.matmul(po[3][:], onerow[:], row(RR),
                          start=True, stop=True).then_inc(S("p5bc"), 1)

    @block.scalar
    def _(scalar):
        scalar.wait_ge(PH, ph0)
        for ts in range(NT5):
            for hf in range(2):
                half = 2 * ts + hf
                for o in range(8):
                    scalar.wait_ge(S("p5mm"), (half + 1) * NKB)
                    oo = hf * 8 + o
                    scalar.activation(ff[:, oo * 512:(oo + 1) * 512],
                                      po[o][:], AF.Copy)
                    scalar.activation(sq[:, oo * 512:(oo + 1) * 512],
                                      po[o][:],
                                      AF.Square).then_inc(S("p5ev"), 1)
            scalar.wait_ge(S("p5rv"), ts + 1)
            scalar.activation(row(RT), row(RV), AF.Ln)
            scalar.activation(row(RR), row(RT), AF.Exp,
                              scale=-0.5).then_inc(S("p5rs"), 1)

    @block.vector
    def _(vector):
        vector.wait_ge(PH, ph0)
        for ts in range(NT5):
            for hf in range(2):
                for o in range(8):
                    oo = hf * 8 + o
                    vector.wait_ge(S("p5ev"), 16 * ts + oo + 1)
                    vector.tensor_copy(cb[:, oo * 512:(oo + 1) * 512],
                                       ff[:, oo * 512:(oo + 1) * 512]
                                       ).then_inc(S("p5cb"), 1)
            vector.wait_ge(S("p5st"), ts + 1)
            vector.tensor_scalar(row(RM), po[0][0:1, :], 1.0 / DM, None,
                                 ALU.mult)
            vector.tensor_scalar(row(RV), po[1][0:1, :], 1.0 / DM, None,
                                 ALU.mult)
            vector.tensor_tensor(row(RT), row(RM), row(RM), ALU.mult)
            vector.tensor_tensor(row(RV), row(RV), row(RT), ALU.subtract)
            vector.tensor_scalar(row(RV), row(RV), EPS, None,
                                 ALU.add).then_inc(S("p5rv"), 1)
            vector.wait_ge(S("p5bc"), ts + 1)
            vector.wait_ge(S("p5x2"), 16 * (ts + 1))
            if ts > 0:
                vector.wait_ge(S("p5out"), 16 * ts)
            for oo in range(NKD):
                osl = slice(oo * 512, (oo + 1) * 512)
                vector.tensor_tensor(obuf[:, osl], ff[:, osl], po[2][:],
                                     ALU.subtract)
                vector.tensor_tensor(obuf[:, osl], obuf[:, osl], po[3][:],
                                     ALU.mult)
                vector.tensor_tensor(obuf[:, osl], obuf[:, osl], x2t[:, osl],
                                     ALU.add).then_inc(S("p5x3"), 1)

    p5.close()
    return ph0
